# version 01
# baseline (speedup 1.0000x reference)
import numpy as np
from contextlib import ExitStack
import concourse.bass as bass
import concourse.mybir as mybir
from concourse.bass_utils import run_bass_kernel_spmd

F32 = mybir.dt.float32
BF16 = mybir.dt.bfloat16
AF = mybir.ActivationFunctionType
ALU = mybir.AluOpType
AXX = mybir.AxisListType.X

NCORE = 8
NSEQ = 2
L = 2048
D = 1024
DC = 8
H = 4096
HC = 32
NE = 8
NT = 512
NQ = NSEQ * L // NT
TCH = 512
NTC = L // TCH
NST = 16
CC = 16
EPS = 1e-5

DEBUG_TAPS = False
SERIAL_TC = 0
LAUNDER_XN = False

ENGS = ("sp", "pe", "dve", "act", "pool")
QSP = tuple(f"qsp{i}" for i in range(48))
QPL = tuple(f"qpl{i}" for i in range(24))
SEMS = ("sp", "pe", "dve", "act", "pool") + QSP + QPL


class Buf:
    __slots__ = ("w", "rs")

    def __init__(self):
        self.w = None
        self.rs = []


class Emit:

    def __init__(self, sems):
        self.sems = sems
        self.q = {e: [] for e in ENGS}
        self.cnt = {s: 0 for s in SEMS}
        self.lastw = {e: {s: 0 for s in SEMS} for e in ENGS}
        self.oddi = {"sp": 0, "pool": 0, "pe": 0, "dve": 0, "act": 0}
        self.pools = {"sp": QSP, "pool": QPL}

    def _wait(self, eng, s, c):
        if c <= 0 or self.lastw[eng][s] >= c:
            return
        assert c <= self.cnt[s], f"wait-on-future {s}:{c}>{self.cnt[s]}"
        self.lastw[eng][s] = c
        h = self.sems[s]
        self.q[eng].append(lambda e, h=h, c=c: e.wait_ge(h, c))

    def op(self, eng, fn, reads=(), writes=(), dma=False, sem_override=None,
           fifo_waw=False):
        need_raw = {}
        need_war = {}

        def add(d, tok):
            if tok is None:
                return
            s, c = tok
            if d.get(s, 0) < c:
                d[s] = c

        for b in reads:
            add(need_raw, b.w)
        for b in writes:
            add(need_war, b.w)
            for t in b.rs:
                add(need_war, t)
        own_pool = self.pools.get(eng, ()) if dma else ()
        for s in sorted(need_raw):
            self._wait(eng, s, need_raw[s])
        for s in sorted(need_war):
            if fifo_waw and s in own_pool:
                continue
            self._wait(eng, s, need_war[s])
        s = sem_override if sem_override else eng
        inc = 16 if dma else 1
        self.cnt[s] += inc
        c = self.cnt[s]
        h = self.sems[s]
        self.q[eng].append(lambda e, fn=fn, h=h, inc=inc: fn(e).then_inc(h, inc))
        tok = (s, c)
        for b in reads:
            b.rs.append(tok)
        for b in writes:
            b.w = tok
            b.rs = []
        return tok

    def raw(self, eng, fn):
        self.q[eng].append(fn)

    def dma(self, eng, out_ap, in_ap, reads=(), writes=(), odd=False,
            fifo_waw=False, **kw):
        pool = self.pools[eng]
        so = pool[self.oddi[eng] % len(pool)]
        self.oddi[eng] += 1
        return self.op(eng, lambda e: e.dma_start(out=out_ap, in_=in_ap, **kw),
                       reads, writes, dma=True, sem_override=so,
                       fifo_waw=fifo_waw)

    def barrier(self):
        snap = dict(self.cnt)
        for e in ENGS:
            for s in SEMS:
                self._wait(e, s, snap[s])

    def flush(self, block):
        secs = {"sp": block.sync, "pe": block.tensor, "dve": block.vector,
                "act": block.scalar, "pool": block.gpsimd}
        for e in ENGS:
            ops = self.q[e]
            if not ops:
                continue

            def run(engine, ops=ops):
                for f in ops:
                    f(engine)

            secs[e](run)
            self.q[e] = []


def build_program():
    nc = bass.Bass("TRN2", target_bir_lowering=False)

    def din(name, shape, dt=F32):
        return nc.dram_tensor(name, shape, dt, kind="ExternalInput")

    x_in = din("x_in", [NSEQ, D, L])
    w1t = din("w1t", [NE, D, H])
    w2t = din("w2t", [NE, H, D])
    b1t = din("b1t", [128, NE * HC])
    b2d = din("b2d", [NE, D])
    gwt = din("gwt", [128, DC * NE])
    gbd = din("gbd", [NE, 1])
    convl = din("convl", [CC, 4, 128, 128])
    avgl = din("avgl", [2, 128, 128])
    xpj = din("xpj", [CC, 128, 96])
    dpj = din("dpj", [64, D])
    dtbd = din("dtbd", [128, DC])
    alogd = din("alogd", [128, 128])
    dvecd = din("dvecd", [128, DC])
    cvbd = din("cvbd", [128, CC])
    idend = din("idend", [128, 128])

    x_cur = nc.dram_tensor("x_cur", [NSEQ, D, L], F32, kind="Internal")
    xn_sc = nc.dram_tensor("xn_sc", [D, NSEQ * L], BF16, kind="Internal")
    y_sc = nc.dram_tensor("y_sc", [D, NSEQ * L], F32, kind="Internal")
    msk_sc = nc.dram_tensor("msk_sc", [NE, NSEQ * L], F32, kind="Internal")
    bc_sc = nc.dram_tensor("bc_sc", [2, 32, TCH], F32, kind="Internal")
    xnm_dr = nc.dram_tensor("xnm_dr", [128, DC * (L + 4)], BF16, kind="Internal")
    xnm2_dr = nc.dram_tensor("xnm2_dr", [128, DC * (L + 4)], BF16, kind="Internal")
    y_out = nc.dram_tensor("y_out", [NSEQ, D, L], F32, kind="ExternalOutput")
    if DEBUG_TAPS:
        dbg1 = nc.dram_tensor("dbg1", [NSEQ, D, L], F32, kind="ExternalOutput")
        dbg2 = nc.dram_tensor("dbg2", [NSEQ, D, L], F32, kind="ExternalOutput")
        dbg_xc = nc.dram_tensor("dbg_xc", [128, CC * TCH], BF16, kind="ExternalOutput")
        dbg_dt = nc.dram_tensor("dbg_dt", [128, DC * TCH], F32, kind="ExternalOutput")
        dbg_dtx = nc.dram_tensor("dbg_dtx", [128, DC * TCH], F32, kind="ExternalOutput")
        dbg_ya = nc.dram_tensor("dbg_ya", [128, DC * TCH], F32, kind="ExternalOutput")
        dbg_xnm = nc.dram_tensor("dbg_xnm", [128, DC * (L + 4)], BF16, kind="ExternalOutput")
        dbg_cl = nc.dram_tensor("dbg_cl", [128, CC * 4 * 128], BF16, kind="ExternalOutput")
        dbg_ysc = nc.dram_tensor("dbg_ysc", [D, NSEQ * L], F32, kind="ExternalOutput")

    es = ExitStack()
    with es:
        sems = {s: es.enter_context(nc.semaphore(s)) for s in SEMS}
        k = Emit(sems)

        def sb(name, shape, dt):
            return es.enter_context(nc.sbuf_tensor(name, shape, dt))

        gw_sb = sb("gw_sb", [128, DC * NE], F32)
        gb_sb = sb("gb_sb", [NE, 1], F32)
        b1_sb = sb("b1_sb", [128, NE * HC], F32)
        b2_sb = sb("b2_sb", [NE, D], F32)
        iden_sb = sb("iden_sb", [128, 128], F32)
        ones_row = sb("ones_row", [1, 128], F32)
        ones_col = sb("ones_col", [128, 1], F32)
        eps_sb = sb("eps_sb", [1, 1], F32)
        zrow_sb = sb("zrow_sb", [1, 512], BF16)
        scrap_sb = sb("scrap_sb", [1, 8], F32)

        pb = [es.enter_context(nc.psum_tensor(f"pb{i}", [128, 512], F32)) for i in range(8)]
        pB = [Buf() for _ in range(8)]

        bx_cur = Buf()
        bxn_sc = Buf()
        by_sc = Buf()
        bmsk_sc = Buf()
        bbc_sc = [Buf(), Buf()]
        by_out = Buf()
        binit = Buf()

        block = es.enter_context(nc.Block())

        k.dma("sp", gw_sb[:], gwt.ap(), writes=[binit])
        k.dma("sp", gb_sb[:], gbd.ap(), writes=[binit], odd=True)
        k.dma("sp", b1_sb[:], b1t.ap(), writes=[binit])
        k.dma("sp", b2_sb[:], b2d.ap(), writes=[binit], odd=True)
        k.dma("sp", iden_sb[:], idend.ap(), writes=[binit])
        k.op("dve", lambda e: e.memset(ones_row[:], 1.0), writes=[binit])
        k.op("dve", lambda e: e.memset(ones_col[:], 1.0), writes=[binit])
        k.op("dve", lambda e: e.memset(eps_sb[:], EPS), writes=[binit])
        k.op("dve", lambda e: e.memset(zrow_sb[:], 0.0), writes=[binit])



        bscrap = Buf()

        def flush_act(buf):
            k.op("act", lambda e: e.activation(out=scrap_sb[0:1, 0:1], in_=eps_sb[:],
                                               func=AF.Copy, bias=0.0),
                 reads=[binit], writes=[buf])

        def flush_dve(buf):
            k.op("dve", lambda e: e.tensor_copy(scrap_sb[0:1, 1:2], eps_sb[:]),
                 reads=[binit], writes=[buf])

        def fp32_drain(bank_idx, out_ap, M, N):
            k.op("pe", lambda e, a=out_ap, M=M, N=N: e.matmul(
                a, zrow_sb[0:1, 0:M], zrow_sb[0:1, 0:N], start=False, stop=True),
                reads=[binit], writes=[pB[bank_idx]])
        k.flush(block)

        def ffn_call(IN, OUT, bIN, bOUT, tag):
            fes = ExitStack()
            with fes:
                def fsb(name, shape, dt):
                    return fes.enter_context(nc.sbuf_tensor(name + tag, shape, dt))

                w1_sb = fsb("w1_sb", [128, DC * H], BF16)
                w2_sb = fsb("w2_sb", [128, HC * D], BF16)
                h_sb = fsb("h_sb", [128, HC * NT], BF16)
                xnq_sb = [fsb(f"xnq{i}", [128, DC * NT], BF16) for i in range(2)]
                shA = fsb("shA", [128, NT], F32)
                shB = fsb("shB", [128, NT], F32)
                tmp1_sb = fsb("tmp1", [128, NT], F32)
                xb_sb = [fsb(f"xb{i}", [128, NT], F32) for i in range(2)]
                mask_sb = [fsb(f"mask{i}", [128, NT], F32) for i in range(2)]
                sq_sb = shA
                xn32_sb = [shB]
                out_sb = [shB]
                tmp_sb = [shB, tmp1_sb]

                r8_sb = fsb("r8", [NE, NT], F32)
                lg_sb = r8_sb[:]
                msk8_2 = [r8_sb[:]] * 2
                yb_sb = [shA] * 2
                mrow_sb = r8_sb[:]
                msk8_sb = r8_sb[:]
                ltok_sb = fsb("ltok_sb", [128, 4 * NE], F32)
                mx_sb = fsb("mx_sb", [128, 1], F32)
                iq_sb = fsb("iq_sb", [128, NE], F32)
                iq2_sb = fsb("iq2_sb", [128, NE + 1], F32)
                keep_sb = fsb("keep_sb", [128, NE], F32)
                row = {"m": fsb("row_m", [1, NT], F32)[:],
                       "e2": fsb("row_e2", [1, NT], F32)[:],
                       "aa": fsb("row_aa", [1, NT], F32)[:],
                       "bb": fsb("row_bb", [1, NT], F32)[:],
                       "mm": pb[0][0:1, :], "var": pb[1][0:1, :], "ln": pb[0][0:1, :]}

                bw1 = Buf(); bw2 = Buf(); bh = Buf()
                bxnq = [Buf(), Buf()]
                bshA = Buf(); bshB = Buf(); btmp1 = Buf()
                bmask = [Buf(), Buf()]
                bsq = bshA
                byb = [bshA] * 2
                bxn32 = [bshB]
                bout = [bshB]
                btmp = [bshB, btmp1]
                bxb = [Buf(), Buf()]
                br8 = Buf()
                bmsk8_2 = [br8] * 2
                blg = br8; bmrow = br8; bmsk8 = br8
                bltok = Buf(); bmx = Buf(); biq = Buf(); biq2 = Buf()
                bkeep = Buf()
                brow = {"m": Buf(), "e2": Buf(), "aa": Buf(), "bb": Buf(),
                        "mm": pB[0], "var": pB[1], "ln": pB[0]}

                k.op("dve", lambda e: e.memset(iq2_sb[:, 0:1], 0.0), writes=[biq2])

                for q in range(NQ):
                    seq, t0 = q // (L // NT), (q % (L // NT)) * NT
                    qoff = seq * L + t0
                    for c in range(DC):
                        xb, bxbc = xb_sb[c % 2], bxb[c % 2]
                        k.dma("sp", xb[:], IN.ap()[seq, c * 128:(c + 1) * 128, t0:t0 + NT],
                              reads=[bIN], writes=[bxbc])
                        k.op("act", lambda e, xb=xb: e.activation(out=sq_sb[:], in_=xb[:],
                                                                  func=AF.Square),
                             reads=[bxbc], writes=[bsq])
                        flush_act(bsq)
                        k.op("pe", lambda e, xb=xb, c=c: e.matmul(
                            pb[6][0:1, :], ones_col[:], xb[:], start=(c == 0), stop=(c == DC - 1)),
                            reads=[bxbc, binit], writes=[pB[6]])
                        k.op("pe", lambda e, c=c: e.matmul(
                            pb[7][0:1, :], ones_col[:], sq_sb[:], start=(c == 0), stop=(c == DC - 1)),
                            reads=[bsq, binit], writes=[pB[7]])
                    fp32_drain(6, pb[6][0:1, :], 1, NT)
                    fp32_drain(7, pb[7][0:1, :], 1, NT)
                    k.op("dve", lambda e: e.tensor_scalar_mul(row["m"], pb[6][0:1, :], 1.0 / D),
                         reads=[pB[6]], writes=[brow["m"]])
                    k.op("dve", lambda e: e.tensor_scalar_mul(row["e2"], pb[7][0:1, :], 1.0 / D),
                         reads=[pB[7]], writes=[brow["e2"]])
                    k.op("act", lambda e: e.activation(out=row["mm"], in_=row["m"],
                                                       func=AF.Square),
                         reads=[brow["m"]], writes=[brow["mm"]])
                    k.op("dve", lambda e: e.tensor_sub(row["var"], row["e2"], row["mm"]),
                         reads=[brow["e2"], brow["mm"]], writes=[brow["var"]])
                    k.op("act", lambda e: e.activation(out=row["ln"], in_=row["var"],
                                                       func=AF.Ln, bias=eps_sb[:]),
                         reads=[brow["var"]], writes=[brow["ln"]])
                    k.op("act", lambda e: e.activation(out=row["aa"], in_=row["ln"],
                                                       func=AF.Exp, scale=-0.5),
                         reads=[brow["ln"]], writes=[brow["aa"]])
                    k.op("dve", lambda e: e.scalar_tensor_tensor(
                        row["bb"], row["m"], -1.0, row["aa"], ALU.mult, ALU.mult),
                        reads=[brow["m"], brow["aa"]], writes=[brow["bb"]])
                    flush_act(brow["aa"])
                    flush_dve(brow["bb"])
                    k.op("pe", lambda e: e.matmul(pb[6][:], ones_row[:], row["aa"],
                                                  start=True, stop=True),
                         reads=[brow["aa"], binit], writes=[pB[6]])
                    k.op("pe", lambda e: e.matmul(pb[7][:], ones_row[:], row["bb"],
                                                  start=True, stop=True),
                         reads=[brow["bb"], binit], writes=[pB[7]])
                    fp32_drain(6, pb[6][:], 128, NT)
                    fp32_drain(7, pb[7][:], 128, NT)
                    for c in range(DC):
                        xb, bxbc = xb_sb[c % 2], bxb[c % 2]
                        x32, bx32 = xn32_sb[0], bxn32[0]
                        k.dma("sp", xb[:], IN.ap()[seq, c * 128:(c + 1) * 128, t0:t0 + NT],
                              reads=[bIN], writes=[bxbc])
                        k.op("dve", lambda e, xb=xb, x32=x32: e.tensor_mul(x32[:], xb[:], pb[6][:]),
                             reads=[bxbc, pB[6]], writes=[bx32])
                        k.op("dve", lambda e, x32=x32: e.tensor_add(x32[:], x32[:], pb[7][:]),
                             reads=[pB[7], bx32], writes=[bx32])
                        flush_dve(bx32)
                        k.op("act", lambda e, x32=x32, c=c: e.activation(
                            out=xnq_sb[0][:, c * NT:(c + 1) * NT], in_=x32[:], func=AF.Copy, bias=0.0),
                            reads=[bx32], writes=[bxnq[0]])
                        k.op("pe", lambda e, x32=x32, c=c: e.matmul(
                            pb[5][0:NE, :], gw_sb[:, c * NE:(c + 1) * NE], x32[:],
                            start=(c == 0), stop=(c == DC - 1)),
                            reads=[bx32, binit], writes=[pB[5]])
                    flush_act(bxnq[0])
                    k.dma("sp",
                          xn_sc.ap().rearrange("(c p) t -> p c t", p=128)[:, :, qoff:qoff + NT],
                          xnq_sb[0][:].rearrange("p (c t) -> p c t", t=NT),
                          reads=[bxnq[0]], writes=[bxn_sc])
                    fp32_drain(5, pb[5][0:NE, :], NE, NT)
                    k.op("dve", lambda e: e.tensor_scalar_add(lg_sb, pb[5][0:NE, :], gb_sb[:]),
                         reads=[pB[5], binit], writes=[blg])
                    flush_dve(blg)
                    for s in range(4):
                        k.op("pe", lambda e, s=s: e.transpose(
                            pb[4][:, s * NE:(s + 1) * NE], lg_sb[:, s * 128:(s + 1) * 128],
                            iden_sb[0:NE, 0:NE]),
                            reads=[blg, binit], writes=[pB[4]])
                        fp32_drain(4, pb[4][:, s * NE:(s + 1) * NE], 128, NE)
                        k.op("dve", lambda e, s=s: e.tensor_copy(
                            ltok_sb[:, s * NE:(s + 1) * NE], pb[4][:, s * NE:(s + 1) * NE]),
                            reads=[pB[4]], writes=[bltok])
                    for s in range(4):
                        sl = slice(s * NE, (s + 1) * NE)
                        k.op("dve", lambda e, sl=sl: e.reduce_max(mx_sb[:], ltok_sb[:, sl], axis=AXX),
                             reads=[bltok], writes=[bmx])
                        k.op("dve", lambda e, sl=sl: e.tensor_scalar(
                            iq_sb[:], ltok_sb[:, sl], mx_sb[:], None, ALU.is_equal),
                            reads=[bltok, bmx], writes=[biq])
                        k.op("dve", lambda e: e.tensor_tensor_scan(
                            iq2_sb[:, 1:NE + 1], iq_sb[:], iq_sb[:], 0.0, ALU.max, ALU.max),
                            reads=[biq], writes=[biq2])
                        k.op("dve", lambda e: e.tensor_sub(keep_sb[:], iq2_sb[:, 1:NE + 1],
                                                           iq2_sb[:, 0:NE]),
                             reads=[biq2], writes=[bkeep])
                        flush_dve(bkeep)
                        k.op("pe", lambda e, s=s: e.transpose(
                            pb[4][0:NE, s * 128:(s + 1) * 128], keep_sb[:], iden_sb[:]),
                            reads=[bkeep, binit], writes=[pB[4]])
                        fp32_drain(4, pb[4][0:NE, s * 128:(s + 1) * 128], NE, 128)
                        k.op("dve", lambda e, s=s: e.tensor_copy(
                            mrow_sb[:, s * 128:(s + 1) * 128], pb[4][0:NE, s * 128:(s + 1) * 128]),
                            reads=[pB[4]], writes=[bmrow])
                    flush_dve(bmrow)
                    k.dma("sp", msk_sc.ap()[:, qoff:qoff + NT], mrow_sb,
                          reads=[bmrow], writes=[bmsk_sc], odd=True)

                k.barrier()
                for e_ in range(NE):
                    k.barrier()
                    for kk in range(DC):
                        k.dma("pool", w1_sb[:, kk * H:(kk + 1) * H],
                              w1t.ap()[e_, kk * 128:(kk + 1) * 128, :], writes=[bw1])
                    for j4 in range(8):
                        k.dma("pool",
                              w2_sb[:, j4 * 4 * D:(j4 + 1) * 4 * D].rearrange(
                                  "p (c d) -> p c d", d=D),
                              w2t.ap()[e_].rearrange("(a c p) d -> a p c d", a=8, p=128)[j4],
                              writes=[bw2])
                    for q in range(NQ):
                        seq, t0 = q // (L // NT), (q % (L // NT)) * NT
                        qoff = seq * L + t0
                        xq, bxq = xnq_sb[q % 2], bxnq[q % 2]
                        xn_view = xn_sc.ap().rearrange("(c p) t -> p c t", p=128)
                        for half in range(2):
                            k.dma("sp",
                                  xq[:, half * 4 * NT:(half + 1) * 4 * NT].rearrange(
                                      "p (c t) -> p c t", t=NT),
                                  xn_view[:, 4 * half:4 * half + 4, qoff:qoff + NT],
                                  reads=[bxn_sc], writes=[bxq])
                        msk, bmsk = mask_sb[q % 2], bmask[q % 2]
                        k.dma("sp", msk[:],
                              msk_sc.ap()[e_:e_ + 1, qoff:qoff + NT].partition_broadcast(128).squeeze(1),
                              reads=[bmsk_sc], writes=[bmsk], odd=True)
                        for j in range(HC):
                            bank = j % 4
                            k.op("pe", lambda e, j=j, xq=xq, bank=bank: [
                                e.matmul(pb[bank][:], w1_sb[:, kk * H + j * 128:kk * H + (j + 1) * 128],
                                         xq[:, kk * NT:(kk + 1) * NT],
                                         start=(kk == 0), stop=(kk == DC - 1))
                                for kk in range(DC)][-1],
                                reads=[bw1, bxq], writes=[pB[bank]])
                            k.op("act", lambda e, j=j, e_=e_, bank=bank: e.activation(
                                out=h_sb[:, j * NT:(j + 1) * NT], in_=pb[bank][:], func=AF.Relu,
                                bias=b1_sb[:, e_ * HC + j:e_ * HC + j + 1]),
                                reads=[pB[bank], binit], writes=[bh])
                        flush_act(bh)
                        for c in range(DC):
                            bank = 4 + c % 2
                            k.op("pe", lambda e, c=c, bank=bank: [
                                e.matmul(pb[bank][:], w2_sb[:, j * D + c * 128:j * D + (c + 1) * 128],
                                         h_sb[:, j * NT:(j + 1) * NT],
                                         start=(j == 0), stop=(j == HC - 1))
                                for j in range(HC)][-1],
                                reads=[bw2, bh], writes=[pB[bank]])
                            tb, btb = tmp_sb[c % 2], btmp[c % 2]
                            k.op("dve", lambda e, tb=tb, bank=bank, msk=msk: e.tensor_mul(
                                tb[:], pb[bank][:], msk[:]),
                                reads=[pB[bank], bmsk], writes=[btb])
                            flush_dve(btb)
                            if e_ == 0:
                                k.dma("pool", y_sc.ap()[c * 128:(c + 1) * 128, qoff:qoff + NT], tb[:],
                                      reads=[btb], writes=[by_sc], fifo_waw=True)
                            else:
                                k.dma("pool", y_sc.ap()[c * 128:(c + 1) * 128, qoff:qoff + NT], tb[:],
                                      reads=[btb], writes=[by_sc], accum_op=ALU.add,
                                      fifo_waw=True)

                k.barrier()
                if DEBUG_TAPS and tag == "_f1":
                    k.dma("sp", dbg_ysc.ap(), y_sc.ap(), reads=[by_sc], writes=[Buf()], odd=True)
                for q in range(NQ):
                    seq, t0 = q // (L // NT), (q % (L // NT)) * NT
                    qoff = seq * L + t0
                    m8, bm8 = msk8_2[q % 2], bmsk8_2[q % 2]
                    k.dma("sp", m8[:], msk_sc.ap()[:, qoff:qoff + NT],
                          reads=[bmsk_sc], writes=[bm8], odd=True)
                    for c in range(DC):
                        xb, bxbc = xb_sb[c % 2], bxb[c % 2]
                        yb, bybc = yb_sb[c % 2], byb[c % 2]
                        ob, bobc = out_sb[0], bout[0]
                        k.dma("sp", xb[:], IN.ap()[seq, c * 128:(c + 1) * 128, t0:t0 + NT],
                              reads=[bIN], writes=[bxbc])
                        k.dma("sp", yb[:], y_sc.ap()[c * 128:(c + 1) * 128, qoff:qoff + NT],
                              reads=[by_sc], writes=[bybc])
                        k.op("pe", lambda e, c=c, m8=m8: e.matmul(
                            pb[3][:], b2_sb[:, c * 128:(c + 1) * 128], m8[:],
                            start=True, stop=False),
                            reads=[bm8, binit], writes=[pB[3]])
                        fp32_drain(3, pb[3][:], 128, NT)
                        k.op("dve", lambda e, xb=xb, ob=ob: e.scalar_tensor_tensor(
                            ob[:], pb[3][:], 0.5, xb[:], ALU.mult, ALU.add),
                            reads=[pB[3], bxbc], writes=[bobc])
                        k.op("dve", lambda e, yb=yb, ob=ob: e.scalar_tensor_tensor(
                            ob[:], yb[:], 0.5, ob[:], ALU.mult, ALU.add),
                            reads=[bybc, bobc], writes=[bobc])
                        flush_dve(bobc)
                        k.dma("sp", OUT.ap()[seq, c * 128:(c + 1) * 128, t0:t0 + NT], ob[:],
                              reads=[bobc], writes=[bOUT])
                k.flush(block)

        ffn_call(x_in, x_cur, Buf(), bx_cur, "_f1")
        k.barrier()
        if DEBUG_TAPS:
            k.dma("sp", dbg1.ap(), x_cur.ap(), reads=[bx_cur], writes=[Buf()], odd=True)

        mes = ExitStack()
        with mes:
            def msb(name, shape, dt):
                return mes.enter_context(nc.sbuf_tensor(name, shape, dt))

            cl_sb = msb("cl_sb", [128, CC * 4 * 128], BF16)
            xpj_sb = msb("xpj_sb", [128, CC * 96], BF16)
            dpj_sb = msb("dpj_sb", [64, D], BF16)
            avg_sb = msb("avg_sb", [128, 2 * 128], BF16)
            dtb_sb = msb("dtb_sb", [128, DC], F32)
            negA_sb = msb("negA_sb", [128, 128], F32)
            dvec_sb = msb("dvec_sb", [128, DC], F32)
            cvb_sb = msb("cvb_sb", [128, CC], F32)
            carry_sb = msb("carry_sb", [128, 128], F32)
            xnm_sb = msb("xnm_sb", [128, DC * (L + 4)], BF16)
            xnm2_sb = msb("xnm2_sb", [128, DC * (L + 4)], BF16)
            xc_sb = msb("xc_sb", [128, CC * TCH], BF16)
            dt_sb = msb("dt_sb", [128, DC * TCH], F32)
            dtx_sb = msb("dtx_sb", [128, DC * TCH], F32)
            ya_sb = msb("ya_sb", [128, DC * TCH], F32)
            dtp_sb = msb("dtp_sb", [64, TCH], BF16)
            bcq_sb = msb("bcq_sb", [96, TCH], F32)
            Bb_sb = [msb(f"Bb{i}", [128, TCH], F32) for i in range(2)]
            Cb_sb = [msb(f"Cb{i}", [128, TCH], F32) for i in range(2)]
            dec_sb = [msb(f"dec{i}", [128, TCH], F32) for i in range(2)]
            u_sb = [msb(f"u{i}", [128, TCH], F32) for i in range(2)]
            hh_sb = [msb(f"hh{i}", [128, TCH], F32) for i in range(2)]
            pr_sb = [msb(f"pr{i}", [128, TCH], F32) for i in range(2)]

            xm_sb = [msb(f"xm{i}", [128, TCH], F32) for i in range(2)]

            z_sb = msb("z_sb", [128, TCH], F32)
            yr_sb = dec_sb
            sqm_sb = None
            t1_sb = msb("t1_sb", [128, TCH], F32)
            t2_sb = msb("t2_sb", [128, TCH], F32)
            t3_sb = msb("t3_sb", [128, TCH], F32)
            sqm_sb = t3_sb
            rowm = {"m": msb("rowm_m", [1, TCH], F32)[:],
                    "e2": msb("rowm_e2", [1, TCH], F32)[:],
                    "aa": msb("rowm_aa", [1, TCH], F32)[:],
                    "bb": msb("rowm_bb", [1, TCH], F32)[:],
                    "mm": pb[0][0:1, :], "var": pb[1][0:1, :], "ln": pb[0][0:1, :]}

            bcl = Buf(); bxnm = Buf(); bxnm2 = Buf(); bxc = Buf(); bdt = Buf()
            bdtx = Buf(); bya = Buf(); bdtp = Buf(); bbcq = Buf()
            bBb = [Buf(), Buf()]; bCb = [Buf(), Buf()]
            bdec = [Buf(), Buf()]; bu = [Buf(), Buf()]
            bhh = [Buf(), Buf()]; bpr = [Buf(), Buf()]
            bxm = [Buf(), Buf()]
            bz = Buf(); bt1 = Buf(); bt2 = Buf(); bt3 = Buf()
            bsqm = bt3
            byr = bdec
            browm = {"m": Buf(), "e2": Buf(), "aa": Buf(), "bb": Buf(),
                     "mm": pB[0], "var": pB[1], "ln": pB[0]}
            bcarry = Buf()

            for m in range(CC):
                k.dma("pool",
                      cl_sb[:, m * 512:(m + 1) * 512].rearrange("p (kk c) -> p kk c", c=128),
                      convl.ap()[m].rearrange("k p c -> p k c"), writes=[bcl])
            for ccc in range(CC):
                k.dma("pool", xpj_sb[:, ccc * 96:(ccc + 1) * 96], xpj.ap()[ccc], writes=[binit])
            k.dma("pool", dpj_sb[:], dpj.ap(), writes=[binit], odd=True)
            k.dma("sp", dtb_sb[:], dtbd.ap(), writes=[binit])
            k.dma("sp", dvec_sb[:], dvecd.ap(), writes=[binit])
            k.dma("sp", cvb_sb[:], cvbd.ap(), writes=[binit])
            k.dma("sp", negA_sb[:], alogd.ap(), writes=[binit])
            k.op("act", lambda e: e.activation(out=negA_sb[:], in_=negA_sb[:], func=AF.Exp),
                 reads=[binit], writes=[binit])
            k.op("dve", lambda e: e.tensor_scalar_mul(negA_sb[:], negA_sb[:], -1.0),
                 reads=[binit], writes=[binit])
            k.dma("pool", avg_sb[:, 0:128], avgl.ap()[0], writes=[binit])
            k.dma("pool", avg_sb[:, 128:256], avgl.ap()[1], writes=[binit])

            for seq in range(NSEQ):
                    for c in range(DC):
                        k.op("dve", lambda e, c=c: e.memset(
                            xnm_sb[:, c * (L + 4):c * (L + 4) + 4], 0.0), writes=[bxnm])
                        k.op("dve", lambda e, c=c: e.memset(
                            xnm2_sb[:, c * (L + 4):c * (L + 4) + 4], 0.0), writes=[bxnm2])
                    for tc in range(NTC):
                        t0 = tc * TCH
                        for c in range(DC):
                            xb, bxbc = xm_sb[c % 2], bxm[c % 2]
                            k.dma("sp", xb[:], x_cur.ap()[seq, c * 128:(c + 1) * 128, t0:t0 + TCH],
                                  reads=[bx_cur], writes=[bxbc])
                            k.op("act", lambda e, xb=xb: e.activation(
                                out=sqm_sb[:], in_=xb[:], func=AF.Square),
                                reads=[bxbc], writes=[bsqm])
                            flush_act(bsqm)
                            k.op("pe", lambda e, xb=xb, c=c: e.matmul(
                                pb[6][0:1, :], ones_col[:], xb[:], start=(c == 0), stop=(c == DC - 1)),
                                reads=[bxbc, binit], writes=[pB[6]])
                            k.op("pe", lambda e, c=c: e.matmul(
                                pb[7][0:1, :], ones_col[:], sqm_sb[:], start=(c == 0), stop=(c == DC - 1)),
                                reads=[bsqm, binit], writes=[pB[7]])
                        fp32_drain(6, pb[6][0:1, :], 1, TCH)
                        fp32_drain(7, pb[7][0:1, :], 1, TCH)
                        k.op("dve", lambda e: e.tensor_scalar_mul(rowm["m"], pb[6][0:1, :], 1.0 / D),
                             reads=[pB[6]], writes=[browm["m"]])
                        k.op("dve", lambda e: e.tensor_scalar_mul(rowm["e2"], pb[7][0:1, :], 1.0 / D),
                             reads=[pB[7]], writes=[browm["e2"]])
                        k.op("dve", lambda e: e.tensor_mul(rowm["mm"], rowm["m"], rowm["m"]),
                             reads=[browm["m"]], writes=[browm["mm"]])
                        k.op("dve", lambda e: e.tensor_sub(rowm["var"], rowm["e2"], rowm["mm"]),
                             reads=[browm["e2"], browm["mm"]], writes=[browm["var"]])
                        k.op("act", lambda e: e.activation(out=rowm["ln"], in_=rowm["var"],
                                                           func=AF.Ln, bias=eps_sb[:]),
                             reads=[browm["var"]], writes=[browm["ln"]])
                        k.op("act", lambda e: e.activation(out=rowm["aa"], in_=rowm["ln"],
                                                           func=AF.Exp, scale=-0.5),
                             reads=[browm["ln"]], writes=[browm["aa"]])
                        k.op("dve", lambda e: e.scalar_tensor_tensor(
                            rowm["bb"], rowm["m"], -1.0, rowm["aa"], ALU.mult, ALU.mult),
                            reads=[browm["m"], browm["aa"]], writes=[browm["bb"]])
                        flush_act(browm["aa"])
                        flush_dve(browm["bb"])
                        k.op("pe", lambda e: e.matmul(pb[6][:], ones_row[:], rowm["aa"],
                                                      start=True, stop=True),
                             reads=[browm["aa"], binit], writes=[pB[6]])
                        k.op("pe", lambda e: e.matmul(pb[7][:], ones_row[:], rowm["bb"],
                                                      start=True, stop=True),
                             reads=[browm["bb"], binit], writes=[pB[7]])
                        fp32_drain(6, pb[6][:], 128, TCH)
                        fp32_drain(7, pb[7][:], 128, TCH)
                        for c in range(DC):
                            xb, bxbc = xm_sb[c % 2], bxm[c % 2]
                            k.dma("sp", xb[:], x_cur.ap()[seq, c * 128:(c + 1) * 128, t0:t0 + TCH],
                                  reads=[bx_cur], writes=[bxbc])
                            k.op("dve", lambda e, xb=xb: e.tensor_mul(z_sb[:], xb[:], pb[6][:]),
                                 reads=[bxbc, pB[6]], writes=[bz])
                            co = c * (L + 4) + 4
                            dst = xnm_sb[:, co + t0:co + t0 + TCH]
                            k.op("dve", lambda e, dst=dst: e.tensor_add(dst, z_sb[:], pb[7][:]),
                                 reads=[pB[7], bz], writes=[bxnm])

                    for c in range(DC):
                        co = c * (L + 4) + 4
                        k.op("dve", lambda e, co=co: e.tensor_copy(
                            xnm2_sb[:, co:co + L], xnm_sb[:, co + L - 1:co - 1:-1]),
                            reads=[bxnm], writes=[bxnm2])
                    if LAUNDER_XN:
                        bxd = Buf(); bxd2 = Buf()
                        k.dma("sp", xnm_dr.ap(), xnm_sb[:], reads=[bxnm], writes=[bxd])
                        k.dma("sp", xnm_sb[:], xnm_dr.ap(), reads=[bxd], writes=[bxnm])
                        k.dma("sp", xnm2_dr.ap(), xnm2_sb[:], reads=[bxnm2], writes=[bxd2])
                        k.dma("sp", xnm2_sb[:], xnm2_dr.ap(), reads=[bxd2], writes=[bxnm2])
                    for rev in (0, 1):
                      xnin_sb = xnm2_sb if rev else xnm_sb
                      bxnin = bxnm2 if rev else bxnm
                      for tc in range(NTC):
                        t0 = tc * TCH
                        tcp = tc % 2
                        for m in range(CC):
                            qd = m // 2
                            co = qd * (L + 4) + 1
                            bank = m % 2
                            k.op("pe", lambda e, m=m, co=co, t0=t0, bank=bank, xnin_sb=xnin_sb: [
                                e.matmul(pb[bank][:], cl_sb[:, (m * 4 + kk) * 128:(m * 4 + kk + 1) * 128],
                                         xnin_sb[:, co + t0 + kk:co + t0 + kk + TCH],
                                         start=(kk == 0), stop=(kk == 3))
                                for kk in range(4)][-1],
                                reads=[bcl, bxnin], writes=[pB[bank]])
                            k.op("act", lambda e, m=m, bank=bank: e.activation(
                                out=xc_sb[:, m * TCH:(m + 1) * TCH], in_=pb[bank][:], func=AF.Silu,
                                bias=cvb_sb[:, m:m + 1]),
                                reads=[pB[bank], binit], writes=[bxc])
                        if SERIAL_TC >= 1:
                            k.barrier()
                        flush_act(bxc)
                        k.op("pe", lambda e: [
                            e.matmul(pb[2][0:96, :], xpj_sb[:, ccc * 96:(ccc + 1) * 96],
                                     xc_sb[:, ccc * TCH:(ccc + 1) * TCH],
                                     start=(ccc == 0), stop=(ccc == CC - 1))
                            for ccc in range(CC)][-1],
                            reads=[binit, bxc], writes=[pB[2]])
                        k.op("act", lambda e: e.activation(out=dtp_sb[:], in_=pb[2][0:64, :],
                                                           func=AF.Copy, bias=0.0),
                             reads=[pB[2]], writes=[bdtp])
                        flush_act(bdtp)
                        k.op("dve", lambda e: e.tensor_copy(bcq_sb[64:96, :], pb[2][64:96, :]),
                             reads=[pB[2]], writes=[bbcq])
                        k.dma("sp", bc_sc.ap()[tcp], bcq_sb[64:96, :], reads=[bbcq], writes=[bbc_sc[tcp]], odd=True)
                        for c in range(DC):
                            k.op("pe", lambda e, c=c: e.matmul(
                                pb[3][:], dpj_sb[:, c * 128:(c + 1) * 128], dtp_sb[:],
                                start=True, stop=True),
                                reads=[binit, bdtp], writes=[pB[3]])
                            k.op("dve", lambda e, c=c: e.tensor_scalar_add(
                                z_sb[:], pb[3][:], dtb_sb[:, c:c + 1]),
                                reads=[pB[3], binit], writes=[bz])
                            k.op("act", lambda e: e.activation(out=t1_sb[:], in_=z_sb[:], func=AF.Abs),
                                 reads=[bz], writes=[bt1])
                            k.op("act", lambda e: e.activation(out=t2_sb[:], in_=t1_sb[:],
                                                               func=AF.Exp, scale=-1.0),
                                 reads=[bt1], writes=[bt2])
                            k.op("act", lambda e: e.activation(out=t1_sb[:], in_=t2_sb[:],
                                                               func=AF.Ln, bias=1.0),
                                 reads=[bt2], writes=[bt1])
                            k.op("dve", lambda e: e.tensor_scalar_max(t3_sb[:], z_sb[:], 0.0),
                                 reads=[bz], writes=[bt3])
                            k.op("dve", lambda e, c=c: e.tensor_add(
                                dt_sb[:, c * TCH:(c + 1) * TCH], t3_sb[:], t1_sb[:]),
                                reads=[bt3, bt1], writes=[bdt])
                        for c in range(DC):
                            bank = 4 + c % 2
                            k.op("pe", lambda e, c=c, bank=bank: [
                                e.matmul(pb[bank][:], avg_sb[:, a * 128:(a + 1) * 128],
                                         xc_sb[:, (2 * c + a) * TCH:(2 * c + a + 1) * TCH],
                                         start=(a == 0), stop=(a == 1))
                                for a in range(2)][-1],
                                reads=[binit, bxc], writes=[pB[bank]])
                            k.op("dve", lambda e, c=c, bank=bank: e.tensor_mul(
                                dtx_sb[:, c * TCH:(c + 1) * TCH], dt_sb[:, c * TCH:(c + 1) * TCH],
                                pb[bank][:]),
                                reads=[bdt, pB[bank]], writes=[bdtx])
                            k.op("dve", lambda e, c=c, bank=bank: e.tensor_scalar_mul(
                                ya_sb[:, c * TCH:(c + 1) * TCH], pb[bank][:], dvec_sb[:, c:c + 1]),
                                reads=[pB[bank], binit], writes=[bya])
                        if DEBUG_TAPS and seq == 0 and rev == 0 and tc == 0:
                            k.dma("sp", dbg_cl.ap(), cl_sb[:], reads=[bcl], writes=[Buf()])
                            k.dma("sp", dbg_xc.ap(), xc_sb[:], reads=[bxc], writes=[Buf()])
                            k.dma("sp", dbg_dt.ap(), dt_sb[:], reads=[bdt], writes=[Buf()])
                            k.dma("sp", dbg_dtx.ap(), dtx_sb[:], reads=[bdtx], writes=[Buf()])
                            k.dma("sp", dbg_xnm.ap(), xnm_sb[:], reads=[bxnm], writes=[Buf()])
                        if SERIAL_TC >= 2:
                            k.barrier()
                        for n in range(NST):
                            nb = n % 2
                            k.dma("sp", Bb_sb[nb][:],
                                  bc_sc.ap()[tcp, n:n + 1, :].partition_broadcast(128).squeeze(1),
                                  reads=[bbc_sc[tcp]], writes=[bBb[nb]], odd=True)
                            k.dma("sp", Cb_sb[nb][:],
                                  bc_sc.ap()[tcp, NST + n:NST + n + 1, :].partition_broadcast(128).squeeze(1),
                                  reads=[bbc_sc[tcp]], writes=[bCb[nb]], odd=True)
                            for c in range(DC):
                                i2 = (n * DC + c) % 2
                                dec, bde = dec_sb[i2], bdec[i2]
                                uu, buu = u_sb[i2], bu[i2]
                                hh, bhc = hh_sb[i2], bhh[i2]
                                pr, bprc = pr_sb[i2], bpr[i2]
                                col = c * NST + n
                                k.op("act", lambda e, dec=dec, c=c, col=col: e.activation(
                                    out=dec[:], in_=dt_sb[:, c * TCH:(c + 1) * TCH], func=AF.Exp,
                                    scale=negA_sb[:, col:col + 1]),
                                    reads=[bdt, binit], writes=[bde])
                                k.op("dve", lambda e, uu=uu, c=c, nb=nb: e.tensor_mul(
                                    uu[:], dtx_sb[:, c * TCH:(c + 1) * TCH], Bb_sb[nb][:]),
                                    reads=[bdtx, bBb[nb]], writes=[buu])
                                init = 0.0 if tc == 0 else carry_sb[:, col:col + 1]
                                k.op("dve", lambda e, hh=hh, dec=dec, uu=uu, init=init:
                                     e.tensor_tensor_scan(hh[:], dec[:], uu[:], init,
                                                          ALU.mult, ALU.add),
                                     reads=[bde, buu] + ([bcarry] if tc else []), writes=[bhc])
                                if tc < NTC - 1:
                                    k.op("dve", lambda e, hh=hh, col=col: e.tensor_copy(
                                        carry_sb[:, col:col + 1], hh[:, TCH - 1:TCH]),
                                        reads=[bhc], writes=[bcarry])
                                k.op("pool", lambda e, pr=pr, hh=hh, nb=nb: e.tensor_tensor(
                                    pr[:], hh[:], Cb_sb[nb][:], ALU.mult),
                                    reads=[bhc, bCb[nb]], writes=[bprc])
                                k.dma("pool", ya_sb[:, c * TCH:(c + 1) * TCH], pr[:],
                                      reads=[bprc], writes=[bya], accum_op=ALU.add,
                                      fifo_waw=True)
                        if DEBUG_TAPS and seq == 0 and rev == 0 and tc == 0:
                            k.dma("sp", dbg_ya.ap(), ya_sb[:], reads=[bya], writes=[Buf()])
                        if SERIAL_TC >= 2:
                            k.barrier()
                        for c in range(DC):
                            if rev:
                                yr, byrc = yr_sb[c % 2], byr[c % 2]
                                stop = c * TCH - 1 if c else None
                                k.op("dve", lambda e, yr=yr, c=c, stop=stop: e.tensor_copy(
                                    yr[:], ya_sb[:, (c + 1) * TCH - 1:stop:-1]),
                                    reads=[bya], writes=[byrc])
                                w0 = L - t0 - TCH
                                k.dma("pool", x_cur.ap()[seq, c * 128:(c + 1) * 128, w0:w0 + TCH],
                                      yr[:], reads=[byrc], writes=[bx_cur], accum_op=ALU.add,
                                      fifo_waw=True)
                            else:
                                k.dma("pool", x_cur.ap()[seq, c * 128:(c + 1) * 128, t0:t0 + TCH],
                                      ya_sb[:, c * TCH:(c + 1) * TCH],
                                      reads=[bya], writes=[bx_cur], accum_op=ALU.add,
                                      fifo_waw=True)
            k.flush(block)

        k.barrier()
        if DEBUG_TAPS:
            k.dma("sp", dbg2.ap(), x_cur.ap(), reads=[bx_cur], writes=[Buf()], odd=True)
            k.barrier()

        ffn_call(x_cur, y_out, bx_cur, by_out, "_f2")

        k.barrier()
        k.flush(block)
    return nc


def _host_prep(inputs):
    f32 = np.float32
    w1 = np.asarray(inputs["L0_ffn_w1"], f32)
    w2 = np.asarray(inputs["L0_ffn_w2"], f32)
    b1 = np.asarray(inputs["L0_ffn_b1"], f32)
    b2 = np.asarray(inputs["L0_ffn_b2"], f32)
    gw = np.asarray(inputs["L0_ffn_gate_w"], f32)
    gb = np.asarray(inputs["L0_ffn_gate_b"], f32)
    cw = np.asarray(inputs["L0_attn_conv_w"], f32)
    cb = np.asarray(inputs["L0_attn_conv_b"], f32)
    xp = np.asarray(inputs["L0_attn_x_proj_w"], f32)
    dpw = np.asarray(inputs["L0_attn_dt_proj_w"], f32)
    dpb = np.asarray(inputs["L0_attn_dt_proj_b"], f32)
    alog = np.asarray(inputs["L0_attn_A_log"], f32)
    dv = np.asarray(inputs["L0_attn_D"], f32)

    shared = {}
    shared["w1t"] = np.ascontiguousarray(w1.transpose(0, 2, 1))
    shared["w2t"] = np.ascontiguousarray(w2.transpose(0, 2, 1))
    shared["b1t"] = np.ascontiguousarray(
        b1.reshape(NE, HC, 128).transpose(2, 0, 1).reshape(128, NE * HC))
    shared["b2d"] = b2
    shared["gwt"] = np.ascontiguousarray(
        gw.T.reshape(DC, 128, NE).transpose(1, 0, 2).reshape(128, DC * NE))
    shared["gbd"] = np.ascontiguousarray(gb.reshape(NE, 1))
    convl = np.zeros((CC, 4, 128, 128), f32)
    for m in range(CC):
        c = np.arange(128)
        rr = 64 * (m % 2) + c // 2
        for kk in range(4):
            convl[m, kk, rr, c] = cw[m * 128 + c, 0, kk]
    shared["convl"] = convl
    avglm = np.zeros((2, 128, 128), f32)
    p = np.arange(64)
    avglm[0, 2 * p, p] = 0.5
    avglm[0, 2 * p + 1, p] = 0.5
    p = np.arange(64, 128)
    avglm[1, 2 * p - 128, p] = 0.5
    avglm[1, 2 * p - 127, p] = 0.5
    shared["avgl"] = avglm
    shared["xpj"] = np.ascontiguousarray(xp.T.reshape(CC, 128, 96))
    shared["dpj"] = np.ascontiguousarray(dpw.T)
    shared["dtbd"] = np.ascontiguousarray(dpb.reshape(DC, 128).T)
    shared["alogd"] = np.ascontiguousarray(
        alog.reshape(DC, 128, NST).transpose(1, 0, 2).reshape(128, 128))
    shared["dvecd"] = np.ascontiguousarray(dv.reshape(DC, 128).T)
    shared["cvbd"] = np.ascontiguousarray(cb.reshape(CC, 128).T)
    shared["idend"] = np.eye(128, dtype=f32)
    return shared


def kernel(**inputs):
    x = np.asarray(inputs["L0_attn_x"], np.float32)
    shared = _host_prep(inputs)
    nc = build_program()
    in_maps = []
    for c in range(NCORE):
        shard = np.ascontiguousarray(x[2 * c:2 * c + 2].transpose(0, 2, 1))
        m = dict(shared)
        m["x_in"] = shard
        in_maps.append(m)
    res = run_bass_kernel_spmd(nc, in_maps, core_ids=list(range(NCORE)))
    outs = [res.results[c]["y_out"] for c in range(NCORE)]
    out = np.concatenate(outs, axis=0).reshape(16, D, L)
    return np.ascontiguousarray(out.transpose(0, 2, 1))


# version 02
# speedup vs baseline: 1.0664x; 1.0664x over previous
import numpy as np
from contextlib import ExitStack
import concourse.bass as bass
import concourse.mybir as mybir
from concourse.bass_utils import run_bass_kernel_spmd

F32 = mybir.dt.float32
BF16 = mybir.dt.bfloat16
AF = mybir.ActivationFunctionType
ALU = mybir.AluOpType
AXX = mybir.AxisListType.X

NCORE = 8
NSEQ = 2
L = 2048
D = 1024
DC = 8
H = 4096
HC = 32
NE = 8
NT = 512
NQ = NSEQ * L // NT
TCH = 512
NTC = L // TCH
NST = 16
CC = 16
EPS = 1e-5

DEBUG_TAPS = False
SERIAL_TC = 0
LAUNDER_XN = False

ENGS = ("sp", "pe", "dve", "act", "pool")
QSP = tuple(f"qsp{i}" for i in range(48))
QPL = tuple(f"qpl{i}" for i in range(24))
SEMS = ("sp", "pe", "dve", "act", "pool") + QSP + QPL


class Buf:
    __slots__ = ("w", "rs")

    def __init__(self):
        self.w = None
        self.rs = []


class Emit:

    def __init__(self, sems):
        self.sems = sems
        self.q = {e: [] for e in ENGS}
        self.cnt = {s: 0 for s in SEMS}
        self.lastw = {e: {s: 0 for s in SEMS} for e in ENGS}
        self.oddi = {"sp": 0, "pool": 0, "pe": 0, "dve": 0, "act": 0}
        self.pools = {"sp": QSP, "pool": QPL}

    def _wait(self, eng, s, c):
        if c <= 0 or self.lastw[eng][s] >= c:
            return
        assert c <= self.cnt[s], f"wait-on-future {s}:{c}>{self.cnt[s]}"
        self.lastw[eng][s] = c
        h = self.sems[s]
        self.q[eng].append(lambda e, h=h, c=c: e.wait_ge(h, c))

    def op(self, eng, fn, reads=(), writes=(), dma=False, sem_override=None,
           fifo_waw=False):
        need_raw = {}
        need_war = {}

        def add(d, tok):
            if tok is None:
                return
            s, c = tok
            if d.get(s, 0) < c:
                d[s] = c

        for b in reads:
            add(need_raw, b.w)
        for b in writes:
            add(need_war, b.w)
            for t in b.rs:
                add(need_war, t)
        own_pool = self.pools.get(eng, ()) if dma else ()
        for s in sorted(need_raw):
            self._wait(eng, s, need_raw[s])
        for s in sorted(need_war):
            if fifo_waw and s in own_pool:
                continue
            self._wait(eng, s, need_war[s])
        s = sem_override if sem_override else eng
        inc = 16 if dma else 1
        self.cnt[s] += inc
        c = self.cnt[s]
        h = self.sems[s]
        self.q[eng].append(lambda e, fn=fn, h=h, inc=inc: fn(e).then_inc(h, inc))
        tok = (s, c)
        for b in reads:
            b.rs.append(tok)
        for b in writes:
            b.w = tok
            b.rs = []
        return tok

    def raw(self, eng, fn):
        self.q[eng].append(fn)

    def dma(self, eng, out_ap, in_ap, reads=(), writes=(), odd=False,
            fifo_waw=False, **kw):
        pool = self.pools[eng]
        so = pool[self.oddi[eng] % len(pool)]
        self.oddi[eng] += 1
        return self.op(eng, lambda e: e.dma_start(out=out_ap, in_=in_ap, **kw),
                       reads, writes, dma=True, sem_override=so,
                       fifo_waw=fifo_waw)

    def barrier(self):
        snap = dict(self.cnt)
        for e in ENGS:
            for s in SEMS:
                self._wait(e, s, snap[s])

    def flush(self, block):
        secs = {"sp": block.sync, "pe": block.tensor, "dve": block.vector,
                "act": block.scalar, "pool": block.gpsimd}
        for e in ENGS:
            ops = self.q[e]
            if not ops:
                continue

            def run(engine, ops=ops):
                for f in ops:
                    f(engine)

            secs[e](run)
            self.q[e] = []


def build_program():
    nc = bass.Bass("TRN2", target_bir_lowering=False)

    def din(name, shape, dt=F32):
        return nc.dram_tensor(name, shape, dt, kind="ExternalInput")

    x_in = din("x_in", [NSEQ, D, L])
    w1t = din("w1t", [NE, D, H])
    w2t = din("w2t", [NE, H, D])
    b1t = din("b1t", [128, NE * HC])
    b2d = din("b2d", [NE, D])
    gwt = din("gwt", [128, DC * NE])
    gbd = din("gbd", [NE, 1])
    convl = din("convl", [CC, 4, 128, 128])
    avgl = din("avgl", [2, 128, 128])
    xpj = din("xpj", [CC, 128, 96])
    dpj = din("dpj", [64, D])
    dtbd = din("dtbd", [128, DC])
    alogd = din("alogd", [128, 128])
    dvecd = din("dvecd", [128, DC])
    cvbd = din("cvbd", [128, CC])
    idend = din("idend", [128, 128])

    x_cur = nc.dram_tensor("x_cur", [NSEQ, D, L], F32, kind="Internal")
    xn_sc = nc.dram_tensor("xn_sc", [D, NSEQ * L], BF16, kind="Internal")
    y_sc = nc.dram_tensor("y_sc", [D, NSEQ * L], F32, kind="Internal")
    msk_sc = nc.dram_tensor("msk_sc", [NE, NSEQ * L], F32, kind="Internal")
    bc_sc = nc.dram_tensor("bc_sc", [2, 32, TCH], F32, kind="Internal")
    xnm_dr = nc.dram_tensor("xnm_dr", [128, DC * (L + 4)], BF16, kind="Internal")
    xnm2_dr = nc.dram_tensor("xnm2_dr", [128, DC * (L + 4)], BF16, kind="Internal")
    y_out = nc.dram_tensor("y_out", [NSEQ, D, L], F32, kind="ExternalOutput")
    if DEBUG_TAPS:
        dbg1 = nc.dram_tensor("dbg1", [NSEQ, D, L], F32, kind="ExternalOutput")
        dbg2 = nc.dram_tensor("dbg2", [NSEQ, D, L], F32, kind="ExternalOutput")
        dbg_xc = nc.dram_tensor("dbg_xc", [128, CC * TCH], BF16, kind="ExternalOutput")
        dbg_dt = nc.dram_tensor("dbg_dt", [128, DC * TCH], F32, kind="ExternalOutput")
        dbg_dtx = nc.dram_tensor("dbg_dtx", [128, DC * TCH], F32, kind="ExternalOutput")
        dbg_ya = nc.dram_tensor("dbg_ya", [128, DC * TCH], F32, kind="ExternalOutput")
        dbg_xnm = nc.dram_tensor("dbg_xnm", [128, DC * (L + 4)], BF16, kind="ExternalOutput")
        dbg_cl = nc.dram_tensor("dbg_cl", [128, CC * 4 * 128], BF16, kind="ExternalOutput")
        dbg_ysc = nc.dram_tensor("dbg_ysc", [D, NSEQ * L], F32, kind="ExternalOutput")

    es = ExitStack()
    with es:
        sems = {s: es.enter_context(nc.semaphore(s)) for s in SEMS}
        k = Emit(sems)

        def sb(name, shape, dt):
            return es.enter_context(nc.sbuf_tensor(name, shape, dt))

        gw_sb = sb("gw_sb", [128, DC * NE], F32)
        gb_sb = sb("gb_sb", [NE, 1], F32)
        b1_sb = sb("b1_sb", [128, NE * HC], F32)
        b2_sb = sb("b2_sb", [NE, D], F32)
        iden_sb = sb("iden_sb", [128, 128], F32)
        ones_row = sb("ones_row", [1, 128], F32)
        ones_col = sb("ones_col", [128, 1], F32)
        eps_sb = sb("eps_sb", [1, 1], F32)
        zrow_sb = sb("zrow_sb", [1, 512], BF16)
        scrap_sb = sb("scrap_sb", [1, 8], F32)

        pb = [es.enter_context(nc.psum_tensor(f"pb{i}", [128, 512], F32)) for i in range(8)]
        pB = [Buf() for _ in range(8)]

        bx_cur = Buf()
        bxn_sc = Buf()
        by_sc = Buf()
        bmsk_sc = Buf()
        bbc_sc = [Buf(), Buf()]
        by_out = Buf()
        binit = Buf()

        block = es.enter_context(nc.Block())

        k.dma("sp", gw_sb[:], gwt.ap(), writes=[binit])
        k.dma("sp", gb_sb[:], gbd.ap(), writes=[binit], odd=True)
        k.dma("sp", b1_sb[:], b1t.ap(), writes=[binit])
        k.dma("sp", b2_sb[:], b2d.ap(), writes=[binit], odd=True)
        k.dma("sp", iden_sb[:], idend.ap(), writes=[binit])
        k.op("dve", lambda e: e.memset(ones_row[:], 1.0), writes=[binit])
        k.op("dve", lambda e: e.memset(ones_col[:], 1.0), writes=[binit])
        k.op("dve", lambda e: e.memset(eps_sb[:], EPS), writes=[binit])
        k.op("dve", lambda e: e.memset(zrow_sb[:], 0.0), writes=[binit])



        bscrap = Buf()

        def flush_act(buf):
            k.op("act", lambda e: e.activation(out=scrap_sb[0:1, 0:1], in_=eps_sb[:],
                                               func=AF.Copy, bias=0.0),
                 reads=[binit], writes=[buf])

        def flush_dve(buf):
            k.op("dve", lambda e: e.tensor_copy(scrap_sb[0:1, 1:2], eps_sb[:]),
                 reads=[binit], writes=[buf])

        def fp32_drain(bank_idx, out_ap, M, N):
            k.op("pe", lambda e, a=out_ap, M=M, N=N: e.matmul(
                a, zrow_sb[0:1, 0:M], zrow_sb[0:1, 0:N], start=False, stop=True),
                reads=[binit], writes=[pB[bank_idx]])
        k.flush(block)

        def ffn_call(IN, OUT, bIN, bOUT, tag):
            fes = ExitStack()
            with fes:
                def fsb(name, shape, dt):
                    return fes.enter_context(nc.sbuf_tensor(name + tag, shape, dt))

                w1_sb = fsb("w1_sb", [128, DC * H], BF16)
                w2_sb = fsb("w2_sb", [128, HC * D], BF16)
                h_sb = fsb("h_sb", [128, HC * NT], BF16)
                xnq_sb = [fsb(f"xnq{i}", [128, DC * NT], BF16) for i in range(2)]
                shA = fsb("shA", [128, NT], F32)
                shB = fsb("shB", [128, NT], F32)
                tmp1_sb = fsb("tmp1", [128, NT], F32)
                xb_sb = [fsb(f"xb{i}", [128, NT], F32) for i in range(2)]
                mask_sb = [fsb(f"mask{i}", [128, NT], F32) for i in range(2)]
                sq_sb = shA
                xn32_sb = [shB]
                out_sb = [shB]
                tmp_sb = [shB, tmp1_sb]

                r8_sb = fsb("r8", [NE, NT], F32)
                lg_sb = r8_sb[:]
                msk8_2 = [r8_sb[:]] * 2
                yb_sb = [shA] * 2
                mrow_sb = r8_sb[:]
                msk8_sb = r8_sb[:]
                ltok_sb = fsb("ltok_sb", [128, 4 * NE], F32)
                mx_sb = fsb("mx_sb", [128, 1], F32)
                iq_sb = fsb("iq_sb", [128, NE], F32)
                iq2_sb = fsb("iq2_sb", [128, NE + 1], F32)
                keep_sb = fsb("keep_sb", [128, NE], F32)
                row = {"m": fsb("row_m", [1, NT], F32)[:],
                       "e2": fsb("row_e2", [1, NT], F32)[:],
                       "aa": fsb("row_aa", [1, NT], F32)[:],
                       "bb": fsb("row_bb", [1, NT], F32)[:],
                       "mm": pb[0][0:1, :], "var": pb[1][0:1, :], "ln": pb[0][0:1, :]}

                bw1 = Buf(); bw2 = Buf(); bh = Buf()
                bxnq = [Buf(), Buf()]
                bshA = Buf(); bshB = Buf(); btmp1 = Buf()
                bmask = [Buf(), Buf()]
                bsq = bshA
                byb = [bshA] * 2
                bxn32 = [bshB]
                bout = [bshB]
                btmp = [bshB, btmp1]
                bxb = [Buf(), Buf()]
                br8 = Buf()
                bmsk8_2 = [br8] * 2
                blg = br8; bmrow = br8; bmsk8 = br8
                bltok = Buf(); bmx = Buf(); biq = Buf(); biq2 = Buf()
                bkeep = Buf()
                brow = {"m": Buf(), "e2": Buf(), "aa": Buf(), "bb": Buf(),
                        "mm": pB[0], "var": pB[1], "ln": pB[0]}

                k.op("dve", lambda e: e.memset(iq2_sb[:, 0:1], 0.0), writes=[biq2])

                for q in range(NQ):
                    seq, t0 = q // (L // NT), (q % (L // NT)) * NT
                    qoff = seq * L + t0
                    for c in range(DC):
                        xb, bxbc = xb_sb[c % 2], bxb[c % 2]
                        k.dma("sp", xb[:], IN.ap()[seq, c * 128:(c + 1) * 128, t0:t0 + NT],
                              reads=[bIN], writes=[bxbc])
                        k.op("act", lambda e, xb=xb: e.activation(out=sq_sb[:], in_=xb[:],
                                                                  func=AF.Square),
                             reads=[bxbc], writes=[bsq])
                        flush_act(bsq)
                        k.op("pe", lambda e, xb=xb, c=c: e.matmul(
                            pb[6][0:1, :], ones_col[:], xb[:], start=(c == 0), stop=(c == DC - 1)),
                            reads=[bxbc, binit], writes=[pB[6]])
                        k.op("pe", lambda e, c=c: e.matmul(
                            pb[7][0:1, :], ones_col[:], sq_sb[:], start=(c == 0), stop=(c == DC - 1)),
                            reads=[bsq, binit], writes=[pB[7]])
                    fp32_drain(6, pb[6][0:1, :], 1, NT)
                    fp32_drain(7, pb[7][0:1, :], 1, NT)
                    k.op("dve", lambda e: e.tensor_scalar_mul(row["m"], pb[6][0:1, :], 1.0 / D),
                         reads=[pB[6]], writes=[brow["m"]])
                    k.op("dve", lambda e: e.tensor_scalar_mul(row["e2"], pb[7][0:1, :], 1.0 / D),
                         reads=[pB[7]], writes=[brow["e2"]])
                    k.op("act", lambda e: e.activation(out=row["mm"], in_=row["m"],
                                                       func=AF.Square),
                         reads=[brow["m"]], writes=[brow["mm"]])
                    k.op("dve", lambda e: e.tensor_sub(row["var"], row["e2"], row["mm"]),
                         reads=[brow["e2"], brow["mm"]], writes=[brow["var"]])
                    k.op("act", lambda e: e.activation(out=row["ln"], in_=row["var"],
                                                       func=AF.Ln, bias=eps_sb[:]),
                         reads=[brow["var"]], writes=[brow["ln"]])
                    k.op("act", lambda e: e.activation(out=row["aa"], in_=row["ln"],
                                                       func=AF.Exp, scale=-0.5),
                         reads=[brow["ln"]], writes=[brow["aa"]])
                    k.op("dve", lambda e: e.scalar_tensor_tensor(
                        row["bb"], row["m"], -1.0, row["aa"], ALU.mult, ALU.mult),
                        reads=[brow["m"], brow["aa"]], writes=[brow["bb"]])
                    flush_act(brow["aa"])
                    flush_dve(brow["bb"])
                    k.op("pe", lambda e: e.matmul(pb[6][:], ones_row[:], row["aa"],
                                                  start=True, stop=True),
                         reads=[brow["aa"], binit], writes=[pB[6]])
                    k.op("pe", lambda e: e.matmul(pb[7][:], ones_row[:], row["bb"],
                                                  start=True, stop=True),
                         reads=[brow["bb"], binit], writes=[pB[7]])
                    fp32_drain(6, pb[6][:], 128, NT)
                    fp32_drain(7, pb[7][:], 128, NT)
                    for c in range(DC):
                        xb, bxbc = xb_sb[c % 2], bxb[c % 2]
                        x32, bx32 = xn32_sb[0], bxn32[0]
                        k.dma("sp", xb[:], IN.ap()[seq, c * 128:(c + 1) * 128, t0:t0 + NT],
                              reads=[bIN], writes=[bxbc])
                        k.op("dve", lambda e, xb=xb, x32=x32: e.tensor_mul(x32[:], xb[:], pb[6][:]),
                             reads=[bxbc, pB[6]], writes=[bx32])
                        k.op("dve", lambda e, x32=x32: e.tensor_add(x32[:], x32[:], pb[7][:]),
                             reads=[pB[7], bx32], writes=[bx32])
                        flush_dve(bx32)
                        k.op("act", lambda e, x32=x32, c=c: e.activation(
                            out=xnq_sb[0][:, c * NT:(c + 1) * NT], in_=x32[:], func=AF.Copy, bias=0.0),
                            reads=[bx32], writes=[bxnq[0]])
                        k.op("pe", lambda e, x32=x32, c=c: e.matmul(
                            pb[5][0:NE, :], gw_sb[:, c * NE:(c + 1) * NE], x32[:],
                            start=(c == 0), stop=(c == DC - 1)),
                            reads=[bx32, binit], writes=[pB[5]])
                    flush_act(bxnq[0])
                    k.dma("sp",
                          xn_sc.ap().rearrange("(c p) t -> p c t", p=128)[:, :, qoff:qoff + NT],
                          xnq_sb[0][:].rearrange("p (c t) -> p c t", t=NT),
                          reads=[bxnq[0]], writes=[bxn_sc])
                    fp32_drain(5, pb[5][0:NE, :], NE, NT)
                    k.op("dve", lambda e: e.tensor_scalar_add(lg_sb, pb[5][0:NE, :], gb_sb[:]),
                         reads=[pB[5], binit], writes=[blg])
                    flush_dve(blg)
                    for s in range(4):
                        k.op("pe", lambda e, s=s: e.transpose(
                            pb[4][:, s * NE:(s + 1) * NE], lg_sb[:, s * 128:(s + 1) * 128],
                            iden_sb[0:NE, 0:NE]),
                            reads=[blg, binit], writes=[pB[4]])
                        fp32_drain(4, pb[4][:, s * NE:(s + 1) * NE], 128, NE)
                        k.op("dve", lambda e, s=s: e.tensor_copy(
                            ltok_sb[:, s * NE:(s + 1) * NE], pb[4][:, s * NE:(s + 1) * NE]),
                            reads=[pB[4]], writes=[bltok])
                    for s in range(4):
                        sl = slice(s * NE, (s + 1) * NE)
                        k.op("dve", lambda e, sl=sl: e.reduce_max(mx_sb[:], ltok_sb[:, sl], axis=AXX),
                             reads=[bltok], writes=[bmx])
                        k.op("dve", lambda e, sl=sl: e.tensor_scalar(
                            iq_sb[:], ltok_sb[:, sl], mx_sb[:], None, ALU.is_equal),
                            reads=[bltok, bmx], writes=[biq])
                        k.op("dve", lambda e: e.tensor_tensor_scan(
                            iq2_sb[:, 1:NE + 1], iq_sb[:], iq_sb[:], 0.0, ALU.max, ALU.max),
                            reads=[biq], writes=[biq2])
                        k.op("dve", lambda e: e.tensor_sub(keep_sb[:], iq2_sb[:, 1:NE + 1],
                                                           iq2_sb[:, 0:NE]),
                             reads=[biq2], writes=[bkeep])
                        flush_dve(bkeep)
                        k.op("pe", lambda e, s=s: e.transpose(
                            pb[4][0:NE, s * 128:(s + 1) * 128], keep_sb[:], iden_sb[:]),
                            reads=[bkeep, binit], writes=[pB[4]])
                        fp32_drain(4, pb[4][0:NE, s * 128:(s + 1) * 128], NE, 128)
                        k.op("dve", lambda e, s=s: e.tensor_copy(
                            mrow_sb[:, s * 128:(s + 1) * 128], pb[4][0:NE, s * 128:(s + 1) * 128]),
                            reads=[pB[4]], writes=[bmrow])
                    flush_dve(bmrow)
                    k.dma("sp", msk_sc.ap()[:, qoff:qoff + NT], mrow_sb,
                          reads=[bmrow], writes=[bmsk_sc], odd=True)

                for e_ in range(NE):
                    for kk in range(DC):
                        k.dma("pool", w1_sb[:, kk * H:(kk + 1) * H],
                              w1t.ap()[e_, kk * 128:(kk + 1) * 128, :], writes=[bw1])
                    for j4 in range(8):
                        k.dma("pool",
                              w2_sb[:, j4 * 4 * D:(j4 + 1) * 4 * D].rearrange(
                                  "p (c d) -> p c d", d=D),
                              w2t.ap()[e_].rearrange("(a c p) d -> a p c d", a=8, p=128)[j4],
                              writes=[bw2])
                    for q in range(NQ):
                        seq, t0 = q // (L // NT), (q % (L // NT)) * NT
                        qoff = seq * L + t0
                        xq, bxq = xnq_sb[q % 2], bxnq[q % 2]
                        xn_view = xn_sc.ap().rearrange("(c p) t -> p c t", p=128)
                        for half in range(2):
                            k.dma("sp",
                                  xq[:, half * 4 * NT:(half + 1) * 4 * NT].rearrange(
                                      "p (c t) -> p c t", t=NT),
                                  xn_view[:, 4 * half:4 * half + 4, qoff:qoff + NT],
                                  reads=[bxn_sc], writes=[bxq])
                        msk, bmsk = mask_sb[q % 2], bmask[q % 2]
                        k.dma("sp", msk[:],
                              msk_sc.ap()[e_:e_ + 1, qoff:qoff + NT].partition_broadcast(128).squeeze(1),
                              reads=[bmsk_sc], writes=[bmsk], odd=True)
                        for j in range(HC):
                            bank = j % 4
                            k.op("pe", lambda e, j=j, xq=xq, bank=bank: [
                                e.matmul(pb[bank][:], w1_sb[:, kk * H + j * 128:kk * H + (j + 1) * 128],
                                         xq[:, kk * NT:(kk + 1) * NT],
                                         start=(kk == 0), stop=(kk == DC - 1))
                                for kk in range(DC)][-1],
                                reads=[bw1, bxq], writes=[pB[bank]])
                            k.op("act", lambda e, j=j, e_=e_, bank=bank: e.activation(
                                out=h_sb[:, j * NT:(j + 1) * NT], in_=pb[bank][:], func=AF.Relu,
                                bias=b1_sb[:, e_ * HC + j:e_ * HC + j + 1]),
                                reads=[pB[bank], binit], writes=[bh])
                        flush_act(bh)
                        for c in range(DC):
                            bank = 4 + c % 2
                            k.op("pe", lambda e, c=c, bank=bank: [
                                e.matmul(pb[bank][:], w2_sb[:, j * D + c * 128:j * D + (c + 1) * 128],
                                         h_sb[:, j * NT:(j + 1) * NT],
                                         start=(j == 0), stop=(j == HC - 1))
                                for j in range(HC)][-1],
                                reads=[bw2, bh], writes=[pB[bank]])
                            tb, btb = tmp_sb[c % 2], btmp[c % 2]
                            k.op("dve", lambda e, tb=tb, bank=bank, msk=msk: e.tensor_mul(
                                tb[:], pb[bank][:], msk[:]),
                                reads=[pB[bank], bmsk], writes=[btb])
                            flush_dve(btb)
                            if e_ == 0:
                                k.dma("pool", y_sc.ap()[c * 128:(c + 1) * 128, qoff:qoff + NT], tb[:],
                                      reads=[btb], writes=[by_sc], fifo_waw=True)
                            else:
                                k.dma("pool", y_sc.ap()[c * 128:(c + 1) * 128, qoff:qoff + NT], tb[:],
                                      reads=[btb], writes=[by_sc], accum_op=ALU.add,
                                      fifo_waw=True)

                if DEBUG_TAPS and tag == "_f1":
                    k.dma("sp", dbg_ysc.ap(), y_sc.ap(), reads=[by_sc], writes=[Buf()], odd=True)
                for q in range(NQ):
                    seq, t0 = q // (L // NT), (q % (L // NT)) * NT
                    qoff = seq * L + t0
                    m8, bm8 = msk8_2[q % 2], bmsk8_2[q % 2]
                    k.dma("sp", m8[:], msk_sc.ap()[:, qoff:qoff + NT],
                          reads=[bmsk_sc], writes=[bm8], odd=True)
                    for c in range(DC):
                        xb, bxbc = xb_sb[c % 2], bxb[c % 2]
                        yb, bybc = yb_sb[c % 2], byb[c % 2]
                        ob, bobc = out_sb[0], bout[0]
                        k.dma("sp", xb[:], IN.ap()[seq, c * 128:(c + 1) * 128, t0:t0 + NT],
                              reads=[bIN], writes=[bxbc])
                        k.dma("sp", yb[:], y_sc.ap()[c * 128:(c + 1) * 128, qoff:qoff + NT],
                              reads=[by_sc], writes=[bybc])
                        k.op("pe", lambda e, c=c, m8=m8: e.matmul(
                            pb[3][:], b2_sb[:, c * 128:(c + 1) * 128], m8[:],
                            start=True, stop=False),
                            reads=[bm8, binit], writes=[pB[3]])
                        fp32_drain(3, pb[3][:], 128, NT)
                        k.op("dve", lambda e, xb=xb, ob=ob: e.scalar_tensor_tensor(
                            ob[:], pb[3][:], 0.5, xb[:], ALU.mult, ALU.add),
                            reads=[pB[3], bxbc], writes=[bobc])
                        k.op("dve", lambda e, yb=yb, ob=ob: e.scalar_tensor_tensor(
                            ob[:], yb[:], 0.5, ob[:], ALU.mult, ALU.add),
                            reads=[bybc, bobc], writes=[bobc])
                        flush_dve(bobc)
                        k.dma("sp", OUT.ap()[seq, c * 128:(c + 1) * 128, t0:t0 + NT], ob[:],
                              reads=[bobc], writes=[bOUT])
                k.flush(block)

        ffn_call(x_in, x_cur, Buf(), bx_cur, "_f1")
        k.barrier()
        if DEBUG_TAPS:
            k.dma("sp", dbg1.ap(), x_cur.ap(), reads=[bx_cur], writes=[Buf()], odd=True)

        mes = ExitStack()
        with mes:
            def msb(name, shape, dt):
                return mes.enter_context(nc.sbuf_tensor(name, shape, dt))

            cl_sb = msb("cl_sb", [128, CC * 4 * 128], BF16)
            xpj_sb = msb("xpj_sb", [128, CC * 96], BF16)
            dpj_sb = msb("dpj_sb", [64, D], BF16)
            avg_sb = msb("avg_sb", [128, 2 * 128], BF16)
            dtb_sb = msb("dtb_sb", [128, DC], F32)
            negA_sb = msb("negA_sb", [128, 128], F32)
            dvec_sb = msb("dvec_sb", [128, DC], F32)
            cvb_sb = msb("cvb_sb", [128, CC], F32)
            carry_sb = msb("carry_sb", [128, 128], F32)
            xnm_sb = msb("xnm_sb", [128, DC * (L + 4)], BF16)
            xnm2_sb = msb("xnm2_sb", [128, DC * (L + 4)], BF16)
            xc_sb = msb("xc_sb", [128, CC * TCH], BF16)
            dt_sb = msb("dt_sb", [128, DC * TCH], F32)
            dtx_sb = msb("dtx_sb", [128, DC * TCH], F32)
            ya_sb = msb("ya_sb", [128, DC * TCH], F32)
            dtp_sb = msb("dtp_sb", [64, TCH], BF16)
            bcq_sb = msb("bcq_sb", [96, TCH], F32)
            Bb_sb = [msb(f"Bb{i}", [128, TCH], F32) for i in range(2)]
            Cb_sb = [msb(f"Cb{i}", [128, TCH], F32) for i in range(2)]
            dec_sb = [msb(f"dec{i}", [128, TCH], F32) for i in range(4)]
            u_sb = [msb(f"u{i}", [128, TCH], F32) for i in range(2)]
            hh_sb = [msb(f"hh{i}", [128, TCH], F32) for i in range(4)]
            pr_sb = [msb(f"pr{i}", [128, TCH], F32) for i in range(4)]

            xm_sb = [msb(f"xm{i}", [128, TCH], F32) for i in range(2)]

            z_sb = msb("z_sb", [128, TCH], F32)
            yr_sb = dec_sb
            sqm_sb = None
            t1_sb = msb("t1_sb", [128, TCH], F32)
            t2_sb = u_sb[0]

            t3_sb = msb("t3_sb", [128, TCH], F32)
            sqm_sb = t3_sb
            rowm = {"m": z_sb[0:1, :], "e2": t1_sb[0:1, :],
                    "aa": t3_sb[0:1, :], "bb": dec_sb[0][0:1, :],
                    "mm": pb[0][0:1, :], "var": pb[1][0:1, :], "ln": pb[0][0:1, :]}

            bcl = Buf(); bxnm = Buf(); bxnm2 = Buf(); bxc = Buf(); bdt = Buf()
            bdtx = Buf(); bya = Buf(); bdtp = Buf(); bbcq = Buf()
            bBb = [Buf(), Buf()]; bCb = [Buf(), Buf()]
            bdec = [Buf() for _ in range(4)]; bu = [Buf(), Buf()]
            bhh = [Buf() for _ in range(4)]; bpr = [Buf() for _ in range(4)]
            bxm = [Buf(), Buf()]
            bz = Buf(); bt1 = Buf(); bt3 = Buf()
            bt2 = bu[0]
            bsqm = bt3
            byr = bdec
            browm = {"m": bz, "e2": bt1, "aa": bt3, "bb": bdec[0],
                     "mm": pB[0], "var": pB[1], "ln": pB[0]}
            bcarry = Buf()

            for m in range(CC):
                k.dma("pool",
                      cl_sb[:, m * 512:(m + 1) * 512].rearrange("p (kk c) -> p kk c", c=128),
                      convl.ap()[m].rearrange("k p c -> p k c"), writes=[bcl])
            for ccc in range(CC):
                k.dma("pool", xpj_sb[:, ccc * 96:(ccc + 1) * 96], xpj.ap()[ccc], writes=[binit])
            k.dma("pool", dpj_sb[:], dpj.ap(), writes=[binit], odd=True)
            k.dma("sp", dtb_sb[:], dtbd.ap(), writes=[binit])
            k.dma("sp", dvec_sb[:], dvecd.ap(), writes=[binit])
            k.dma("sp", cvb_sb[:], cvbd.ap(), writes=[binit])
            k.dma("sp", negA_sb[:], alogd.ap(), writes=[binit])
            k.op("act", lambda e: e.activation(out=negA_sb[:], in_=negA_sb[:], func=AF.Exp),
                 reads=[binit], writes=[binit])
            k.op("dve", lambda e: e.tensor_scalar_mul(negA_sb[:], negA_sb[:], -1.0),
                 reads=[binit], writes=[binit])
            k.dma("pool", avg_sb[:, 0:128], avgl.ap()[0], writes=[binit])
            k.dma("pool", avg_sb[:, 128:256], avgl.ap()[1], writes=[binit])

            for seq in range(NSEQ):
                    for c in range(DC):
                        k.op("dve", lambda e, c=c: e.memset(
                            xnm_sb[:, c * (L + 4):c * (L + 4) + 4], 0.0), writes=[bxnm])
                        k.op("dve", lambda e, c=c: e.memset(
                            xnm2_sb[:, c * (L + 4):c * (L + 4) + 4], 0.0), writes=[bxnm2])
                    for tc in range(NTC):
                        t0 = tc * TCH
                        for c in range(DC):
                            xb, bxbc = xm_sb[c % 2], bxm[c % 2]
                            k.dma("sp", xb[:], x_cur.ap()[seq, c * 128:(c + 1) * 128, t0:t0 + TCH],
                                  reads=[bx_cur], writes=[bxbc])
                            k.op("act", lambda e, xb=xb: e.activation(
                                out=sqm_sb[:], in_=xb[:], func=AF.Square),
                                reads=[bxbc], writes=[bsqm])
                            flush_act(bsqm)
                            k.op("pe", lambda e, xb=xb, c=c: e.matmul(
                                pb[6][0:1, :], ones_col[:], xb[:], start=(c == 0), stop=(c == DC - 1)),
                                reads=[bxbc, binit], writes=[pB[6]])
                            k.op("pe", lambda e, c=c: e.matmul(
                                pb[7][0:1, :], ones_col[:], sqm_sb[:], start=(c == 0), stop=(c == DC - 1)),
                                reads=[bsqm, binit], writes=[pB[7]])
                        fp32_drain(6, pb[6][0:1, :], 1, TCH)
                        fp32_drain(7, pb[7][0:1, :], 1, TCH)
                        k.op("dve", lambda e: e.tensor_scalar_mul(rowm["m"], pb[6][0:1, :], 1.0 / D),
                             reads=[pB[6]], writes=[browm["m"]])
                        k.op("dve", lambda e: e.tensor_scalar_mul(rowm["e2"], pb[7][0:1, :], 1.0 / D),
                             reads=[pB[7]], writes=[browm["e2"]])
                        k.op("dve", lambda e: e.tensor_mul(rowm["mm"], rowm["m"], rowm["m"]),
                             reads=[browm["m"]], writes=[browm["mm"]])
                        k.op("dve", lambda e: e.tensor_sub(rowm["var"], rowm["e2"], rowm["mm"]),
                             reads=[browm["e2"], browm["mm"]], writes=[browm["var"]])
                        k.op("act", lambda e: e.activation(out=rowm["ln"], in_=rowm["var"],
                                                           func=AF.Ln, bias=eps_sb[:]),
                             reads=[browm["var"]], writes=[browm["ln"]])
                        k.op("act", lambda e: e.activation(out=rowm["aa"], in_=rowm["ln"],
                                                           func=AF.Exp, scale=-0.5),
                             reads=[browm["ln"]], writes=[browm["aa"]])
                        k.op("dve", lambda e: e.scalar_tensor_tensor(
                            rowm["bb"], rowm["m"], -1.0, rowm["aa"], ALU.mult, ALU.mult),
                            reads=[browm["m"], browm["aa"]], writes=[browm["bb"]])
                        flush_act(browm["aa"])
                        flush_dve(browm["bb"])
                        k.op("pe", lambda e: e.matmul(pb[6][:], ones_row[:], rowm["aa"],
                                                      start=True, stop=True),
                             reads=[browm["aa"], binit], writes=[pB[6]])
                        k.op("pe", lambda e: e.matmul(pb[7][:], ones_row[:], rowm["bb"],
                                                      start=True, stop=True),
                             reads=[browm["bb"], binit], writes=[pB[7]])
                        fp32_drain(6, pb[6][:], 128, TCH)
                        fp32_drain(7, pb[7][:], 128, TCH)
                        for c in range(DC):
                            xb, bxbc = xm_sb[c % 2], bxm[c % 2]
                            k.dma("sp", xb[:], x_cur.ap()[seq, c * 128:(c + 1) * 128, t0:t0 + TCH],
                                  reads=[bx_cur], writes=[bxbc])
                            k.op("dve", lambda e, xb=xb: e.tensor_mul(z_sb[:], xb[:], pb[6][:]),
                                 reads=[bxbc, pB[6]], writes=[bz])
                            co = c * (L + 4) + 4
                            dst = xnm_sb[:, co + t0:co + t0 + TCH]
                            k.op("dve", lambda e, dst=dst: e.tensor_add(dst, z_sb[:], pb[7][:]),
                                 reads=[pB[7], bz], writes=[bxnm])

                    for c in range(DC):
                        co = c * (L + 4) + 4
                        k.op("dve", lambda e, co=co: e.tensor_copy(
                            xnm2_sb[:, co:co + L], xnm_sb[:, co + L - 1:co - 1:-1]),
                            reads=[bxnm], writes=[bxnm2])
                    if LAUNDER_XN:
                        bxd = Buf(); bxd2 = Buf()
                        k.dma("sp", xnm_dr.ap(), xnm_sb[:], reads=[bxnm], writes=[bxd])
                        k.dma("sp", xnm_sb[:], xnm_dr.ap(), reads=[bxd], writes=[bxnm])
                        k.dma("sp", xnm2_dr.ap(), xnm2_sb[:], reads=[bxnm2], writes=[bxd2])
                        k.dma("sp", xnm2_sb[:], xnm2_dr.ap(), reads=[bxd2], writes=[bxnm2])
                    for rev in (0, 1):
                      xnin_sb = xnm2_sb if rev else xnm_sb
                      bxnin = bxnm2 if rev else bxnm
                      for tc in range(NTC):
                        t0 = tc * TCH
                        tcp = tc % 2
                        for m in range(CC):
                            qd = m // 2
                            co = qd * (L + 4) + 1
                            bank = m % 2
                            k.op("pe", lambda e, m=m, co=co, t0=t0, bank=bank, xnin_sb=xnin_sb: [
                                e.matmul(pb[bank][:], cl_sb[:, (m * 4 + kk) * 128:(m * 4 + kk + 1) * 128],
                                         xnin_sb[:, co + t0 + kk:co + t0 + kk + TCH],
                                         start=(kk == 0), stop=(kk == 3))
                                for kk in range(4)][-1],
                                reads=[bcl, bxnin], writes=[pB[bank]])
                            k.op("act", lambda e, m=m, bank=bank: e.activation(
                                out=xc_sb[:, m * TCH:(m + 1) * TCH], in_=pb[bank][:], func=AF.Silu,
                                bias=cvb_sb[:, m:m + 1]),
                                reads=[pB[bank], binit], writes=[bxc])
                        if SERIAL_TC >= 1:
                            k.barrier()
                        flush_act(bxc)
                        k.op("pe", lambda e: [
                            e.matmul(pb[2][0:96, :], xpj_sb[:, ccc * 96:(ccc + 1) * 96],
                                     xc_sb[:, ccc * TCH:(ccc + 1) * TCH],
                                     start=(ccc == 0), stop=(ccc == CC - 1))
                            for ccc in range(CC)][-1],
                            reads=[binit, bxc], writes=[pB[2]])
                        k.op("act", lambda e: e.activation(out=dtp_sb[:], in_=pb[2][0:64, :],
                                                           func=AF.Copy, bias=0.0),
                             reads=[pB[2]], writes=[bdtp])
                        flush_act(bdtp)
                        k.op("dve", lambda e: e.tensor_copy(bcq_sb[64:96, :], pb[2][64:96, :]),
                             reads=[pB[2]], writes=[bbcq])
                        k.dma("sp", bc_sc.ap()[tcp], bcq_sb[64:96, :], reads=[bbcq], writes=[bbc_sc[tcp]], odd=True)
                        for c in range(DC):
                            k.op("pe", lambda e, c=c: e.matmul(
                                pb[3][:], dpj_sb[:, c * 128:(c + 1) * 128], dtp_sb[:],
                                start=True, stop=True),
                                reads=[binit, bdtp], writes=[pB[3]])
                            k.op("dve", lambda e, c=c: e.tensor_scalar_add(
                                z_sb[:], pb[3][:], dtb_sb[:, c:c + 1]),
                                reads=[pB[3], binit], writes=[bz])
                            k.op("act", lambda e: e.activation(out=t1_sb[:], in_=z_sb[:], func=AF.Abs),
                                 reads=[bz], writes=[bt1])
                            k.op("act", lambda e: e.activation(out=t2_sb[:], in_=t1_sb[:],
                                                               func=AF.Exp, scale=-1.0),
                                 reads=[bt1], writes=[bt2])
                            k.op("act", lambda e: e.activation(out=t1_sb[:], in_=t2_sb[:],
                                                               func=AF.Ln, bias=1.0),
                                 reads=[bt2], writes=[bt1])
                            k.op("dve", lambda e: e.tensor_scalar_max(t3_sb[:], z_sb[:], 0.0),
                                 reads=[bz], writes=[bt3])
                            k.op("dve", lambda e, c=c: e.tensor_add(
                                dt_sb[:, c * TCH:(c + 1) * TCH], t3_sb[:], t1_sb[:]),
                                reads=[bt3, bt1], writes=[bdt])
                        for c in range(DC):
                            bank = 4 + c % 2
                            k.op("pe", lambda e, c=c, bank=bank: [
                                e.matmul(pb[bank][:], avg_sb[:, a * 128:(a + 1) * 128],
                                         xc_sb[:, (2 * c + a) * TCH:(2 * c + a + 1) * TCH],
                                         start=(a == 0), stop=(a == 1))
                                for a in range(2)][-1],
                                reads=[binit, bxc], writes=[pB[bank]])
                            k.op("dve", lambda e, c=c, bank=bank: e.tensor_mul(
                                dtx_sb[:, c * TCH:(c + 1) * TCH], dt_sb[:, c * TCH:(c + 1) * TCH],
                                pb[bank][:]),
                                reads=[bdt, pB[bank]], writes=[bdtx])
                            k.op("dve", lambda e, c=c, bank=bank: e.tensor_scalar_mul(
                                ya_sb[:, c * TCH:(c + 1) * TCH], pb[bank][:], dvec_sb[:, c:c + 1]),
                                reads=[pB[bank], binit], writes=[bya])
                        if DEBUG_TAPS and seq == 0 and rev == 0 and tc == 0:
                            k.dma("sp", dbg_cl.ap(), cl_sb[:], reads=[bcl], writes=[Buf()])
                            k.dma("sp", dbg_xc.ap(), xc_sb[:], reads=[bxc], writes=[Buf()])
                            k.dma("sp", dbg_dt.ap(), dt_sb[:], reads=[bdt], writes=[Buf()])
                            k.dma("sp", dbg_dtx.ap(), dtx_sb[:], reads=[bdtx], writes=[Buf()])
                            k.dma("sp", dbg_xnm.ap(), xnm_sb[:], reads=[bxnm], writes=[Buf()])
                        if SERIAL_TC >= 2:
                            k.barrier()
                        for n in range(NST):
                            nb = n % 2
                            k.dma("sp", Bb_sb[nb][:],
                                  bc_sc.ap()[tcp, n:n + 1, :].partition_broadcast(128).squeeze(1),
                                  reads=[bbc_sc[tcp]], writes=[bBb[nb]], odd=True)
                            k.dma("sp", Cb_sb[nb][:],
                                  bc_sc.ap()[tcp, NST + n:NST + n + 1, :].partition_broadcast(128).squeeze(1),
                                  reads=[bbc_sc[tcp]], writes=[bCb[nb]], odd=True)
                            for c in range(DC):
                                i4 = (n * DC + c) % 4
                                i2 = (n * DC + c) % 2
                                dec, bde = dec_sb[i4], bdec[i4]
                                uu, buu = u_sb[i2], bu[i2]
                                hh, bhc = hh_sb[i4], bhh[i4]
                                pr, bprc = pr_sb[i4], bpr[i4]
                                col = c * NST + n
                                k.op("act", lambda e, dec=dec, c=c, col=col: e.activation(
                                    out=dec[:], in_=dt_sb[:, c * TCH:(c + 1) * TCH], func=AF.Exp,
                                    scale=negA_sb[:, col:col + 1]),
                                    reads=[bdt, binit], writes=[bde])
                                k.op("dve", lambda e, uu=uu, c=c, nb=nb: e.tensor_mul(
                                    uu[:], dtx_sb[:, c * TCH:(c + 1) * TCH], Bb_sb[nb][:]),
                                    reads=[bdtx, bBb[nb]], writes=[buu])
                                init = 0.0 if tc == 0 else carry_sb[:, col:col + 1]
                                k.op("dve", lambda e, hh=hh, dec=dec, uu=uu, init=init:
                                     e.tensor_tensor_scan(hh[:], dec[:], uu[:], init,
                                                          ALU.mult, ALU.add),
                                     reads=[bde, buu] + ([bcarry] if tc else []), writes=[bhc])
                                if tc < NTC - 1:
                                    k.op("dve", lambda e, hh=hh, col=col: e.tensor_copy(
                                        carry_sb[:, col:col + 1], hh[:, TCH - 1:TCH]),
                                        reads=[bhc], writes=[bcarry])
                                k.op("pool", lambda e, pr=pr, hh=hh, nb=nb: e.tensor_tensor(
                                    pr[:], hh[:], Cb_sb[nb][:], ALU.mult),
                                    reads=[bhc, bCb[nb]], writes=[bprc])
                                k.dma("pool", ya_sb[:, c * TCH:(c + 1) * TCH], pr[:],
                                      reads=[bprc], writes=[bya], accum_op=ALU.add,
                                      fifo_waw=True)
                        if DEBUG_TAPS and seq == 0 and rev == 0 and tc == 0:
                            k.dma("sp", dbg_ya.ap(), ya_sb[:], reads=[bya], writes=[Buf()])
                        if SERIAL_TC >= 2:
                            k.barrier()
                        for c in range(DC):
                            if rev:
                                yr, byrc = yr_sb[c % 2], byr[c % 2]
                                stop = c * TCH - 1 if c else None
                                k.op("dve", lambda e, yr=yr, c=c, stop=stop: e.tensor_copy(
                                    yr[:], ya_sb[:, (c + 1) * TCH - 1:stop:-1]),
                                    reads=[bya], writes=[byrc])
                                w0 = L - t0 - TCH
                                k.dma("pool", x_cur.ap()[seq, c * 128:(c + 1) * 128, w0:w0 + TCH],
                                      yr[:], reads=[byrc], writes=[bx_cur], accum_op=ALU.add,
                                      fifo_waw=True)
                            else:
                                k.dma("pool", x_cur.ap()[seq, c * 128:(c + 1) * 128, t0:t0 + TCH],
                                      ya_sb[:, c * TCH:(c + 1) * TCH],
                                      reads=[bya], writes=[bx_cur], accum_op=ALU.add,
                                      fifo_waw=True)
            k.flush(block)

        k.barrier()
        if DEBUG_TAPS:
            k.dma("sp", dbg2.ap(), x_cur.ap(), reads=[bx_cur], writes=[Buf()], odd=True)
            k.barrier()

        ffn_call(x_cur, y_out, bx_cur, by_out, "_f2")

        k.barrier()
        k.flush(block)
    return nc


def _host_prep(inputs):
    f32 = np.float32
    w1 = np.asarray(inputs["L0_ffn_w1"], f32)
    w2 = np.asarray(inputs["L0_ffn_w2"], f32)
    b1 = np.asarray(inputs["L0_ffn_b1"], f32)
    b2 = np.asarray(inputs["L0_ffn_b2"], f32)
    gw = np.asarray(inputs["L0_ffn_gate_w"], f32)
    gb = np.asarray(inputs["L0_ffn_gate_b"], f32)
    cw = np.asarray(inputs["L0_attn_conv_w"], f32)
    cb = np.asarray(inputs["L0_attn_conv_b"], f32)
    xp = np.asarray(inputs["L0_attn_x_proj_w"], f32)
    dpw = np.asarray(inputs["L0_attn_dt_proj_w"], f32)
    dpb = np.asarray(inputs["L0_attn_dt_proj_b"], f32)
    alog = np.asarray(inputs["L0_attn_A_log"], f32)
    dv = np.asarray(inputs["L0_attn_D"], f32)

    shared = {}
    shared["w1t"] = np.ascontiguousarray(w1.transpose(0, 2, 1))
    shared["w2t"] = np.ascontiguousarray(w2.transpose(0, 2, 1))
    shared["b1t"] = np.ascontiguousarray(
        b1.reshape(NE, HC, 128).transpose(2, 0, 1).reshape(128, NE * HC))
    shared["b2d"] = b2
    shared["gwt"] = np.ascontiguousarray(
        gw.T.reshape(DC, 128, NE).transpose(1, 0, 2).reshape(128, DC * NE))
    shared["gbd"] = np.ascontiguousarray(gb.reshape(NE, 1))
    convl = np.zeros((CC, 4, 128, 128), f32)
    for m in range(CC):
        c = np.arange(128)
        rr = 64 * (m % 2) + c // 2
        for kk in range(4):
            convl[m, kk, rr, c] = cw[m * 128 + c, 0, kk]
    shared["convl"] = convl
    avglm = np.zeros((2, 128, 128), f32)
    p = np.arange(64)
    avglm[0, 2 * p, p] = 0.5
    avglm[0, 2 * p + 1, p] = 0.5
    p = np.arange(64, 128)
    avglm[1, 2 * p - 128, p] = 0.5
    avglm[1, 2 * p - 127, p] = 0.5
    shared["avgl"] = avglm
    shared["xpj"] = np.ascontiguousarray(xp.T.reshape(CC, 128, 96))
    shared["dpj"] = np.ascontiguousarray(dpw.T)
    shared["dtbd"] = np.ascontiguousarray(dpb.reshape(DC, 128).T)
    shared["alogd"] = np.ascontiguousarray(
        alog.reshape(DC, 128, NST).transpose(1, 0, 2).reshape(128, 128))
    shared["dvecd"] = np.ascontiguousarray(dv.reshape(DC, 128).T)
    shared["cvbd"] = np.ascontiguousarray(cb.reshape(CC, 128).T)
    shared["idend"] = np.eye(128, dtype=f32)
    return shared


def kernel(**inputs):
    x = np.asarray(inputs["L0_attn_x"], np.float32)
    shared = _host_prep(inputs)
    nc = build_program()
    in_maps = []
    for c in range(NCORE):
        shard = np.ascontiguousarray(x[2 * c:2 * c + 2].transpose(0, 2, 1))
        m = dict(shared)
        m["x_in"] = shard
        in_maps.append(m)
    res = run_bass_kernel_spmd(nc, in_maps, core_ids=list(range(NCORE)))
    outs = [res.results[c]["y_out"] for c in range(NCORE)]
    out = np.concatenate(outs, axis=0).reshape(16, D, L)
    return np.ascontiguousarray(out.transpose(0, 2, 1))
